# version 18
# baseline (speedup 1.0000x reference)
import numpy as np
import ml_dtypes
import contextlib
import concourse.bass as bass
import concourse.mybir as mybir
from concourse.bass_utils import run_bass_kernel_spmd

F32 = mybir.dt.float32
BF16 = mybir.dt.bfloat16
AOP = mybir.AluOpType
AF = mybir.ActivationFunctionType
AXX = mybir.AxisListType.X
BF = ml_dtypes.bfloat16

D = 2048
S = 2048
B = 4
NH = 16
HD = 128
FF = 8192
NQ = 8
QR = NQ * 128
NKT = S // 128
KLENS = [2048, 1920, 1792, 1664, 1024, 896, 768, 640]
JS = [k // 128 for k in KLENS]
SLOT_TILES = [[15, 14, 13, 12, 3, 2, 1, 0],
              [11, 10, 9, 8, 7, 6, 5, 4]]
MSK_W = 640
P_OFF = [0]
for k in KLENS:
    P_OFF.append(P_OFF[-1] + k)
P_TOT = P_OFF[-1]
ISQ = float(1.0 / np.sqrt(HD))
SBANKS = ["A0", "A1", "B0", "B1"]

SB = {
    "eye":    (16512,      [128, 128], BF16),
    "stats":  (16768,    [128, 64],  F32),
    "h1_fm":  (17024,   [128, NKT, S // 2], BF16),
    "kvstg":  (49792,  [128, 2048], BF16),
    "q_all":  (82560,  [128, NH, QR], BF16),
    "h1q":    (115328,  [128, NKT, QR], BF16),
    "xt0":    (148096, [128, D], F32),
    "xt1":    (156288, [128, D], F32),
    "xt2":    (164480, [128, D], F32),
    "xt3":    (172672, [128, D], F32),
    "htm0":   (180864, [128, D], BF16),
    "htm1":   (184960, [128, D], BF16),
    "htm2":   (189056, [128, D], BF16),
    "htm3":   (193152, [128, D], BF16),
    "trash":  (197248, [128, D], BF16),
    "wqbuf":  (201344, [128, NKT, 128], BF16),
    "o_fm":   (115328,  [128, NH, QR], BF16),
    "kfm0":   (148096, [128, S], BF16),
    "kfm1":   (152192, [128, S], BF16),
    "vfm":    (156288, [128, S], BF16),
    "vtm0":   (160384, [128, NKT, 128], BF16),
    "vtm1":   (164480, [128, NKT, 128], BF16),
    "p_sb":   (184960, [128, P_TOT], BF16),
    "masks":  (206464, [128, NQ, MSK_W], BF16),
    "diag":   (168576, [128, NQ, 4, 128], BF16),
    "sstat":  (176768, [128, 96], F32),
    "pt0":    (218752, [128, 512], BF16),
    "pt1":    (219776, [128, 512], BF16),
    "pt2":    (177152, [128, 512], BF16),
    "pt3":    (178176, [128, 512], BF16),
    "wkbuf":  (220800, [128, NKT, 128], BF16),
    "wvbuf":  (224896, [128, NKT, 128], BF16),
    "h2_fm":  (17024,   [128, NKT, QR], BF16),
    "woT":    (49792,  [128, NH, D], BF16),
    "xot0":   (148096, [128, D], F32),
    "xot1":   (156288, [128, D], F32),
    "x2t0":   (164480, [128, D], F32),
    "x2t1":   (172672, [128, D], F32),
    "h2tm0":  (180864, [128, D], BF16),
    "h2tm1":  (184960, [128, D], BF16),
    "trash3": (189056, [128, D], BF16),
    "hidden": (49792,  [128, 64, QR], BF16),
    "wg0":    (180864, [128, NKT, 128], BF16),
    "wg1":    (184960, [128, NKT, 128], BF16),
    "wu0":    (189056, [128, NKT, 128], BF16),
    "wu1":    (193152, [128, NKT, 128], BF16),
    "silu":   (197248, [128, QR], BF16),
    "wd0":    (199296, [128, 2, 512], BF16),
    "wd1":    (209536, [128, 2, 512], BF16),
    "wd2":    (211584, [128, 2, 512], BF16),
    "wd3":    (213632, [128, 2, 512], BF16),
    "x2r0":   (201344, [128, 512], F32),
    "x2r1":   (203392, [128, 512], F32),
    "osb0":   (205440, [128, 512], F32),
    "osb1":   (207488, [128, 512], F32),
}
C_NEG, C_AUX, C_AUX2, C_INV = 0, 4, 8, 12
C_EPS = 16
C_GM = 20
C_SG2 = 28
C_RG = 36


ENGINES = ("tensor", "vector", "scalar", "sync", "gpsimd")


def _flat(waits):
    out = []
    for t in waits:
        if t is None:
            continue
        if isinstance(t, list):
            out.extend(x for x in t if x is not None)
        else:
            out.append(t)
    return out


DMA_GROUPS = {"ld": 4, "wld": 4, "st": 2}


class Plan:
    def __init__(self):
        self.ops = {e: [] for e in ENGINES}
        self.sems = {}
        self.cnt = {}
        self.owner = {}
        self.waited = {e: {} for e in ENGINES}
        self.rr = {g: 0 for g in DMA_GROUPS}

    def add_sem(self, name, handle, owner):
        self.sems[name] = handle
        self.cnt[name] = 0
        self.owner[name] = owner

    def dma(self, eng, build, waits, group):
        n = DMA_GROUPS[group]
        i = self.rr[group]
        self.rr[group] = (i + 1) % n
        s = "%s%d" % (group, i)
        pre = self.cnt[s]
        return self.do(eng, build, list(waits) + [(s, pre)], inc=s, k=16)

    def wait(self, eng, tok):
        sem, v = tok
        if v <= 0:
            return
        if self.waited[eng].get(sem, -1) >= v:
            return
        h = self.sems[sem]
        self.ops[eng].append(lambda e, h=h, v=v: e.wait_ge(h, v))
        self.waited[eng][sem] = v

    def do(self, eng, build, waits=(), inc=None, k=1):
        for t in _flat(waits):
            self.wait(eng, t)
        if inc is not None:
            self.cnt[inc] += k
            h = self.sems[inc]
            self.ops[eng].append(lambda e, b=build, h=h, kk=k: b(e).then_inc(h, kk))
            return (inc, self.cnt[inc])
        self.ops[eng].append(lambda e, b=build: b(e))
        return None

    def tok(self, sem):
        return (sem, self.cnt[sem])

    def join(self):
        toks = [self.tok(s) for s in ("pe", "dve", "act")]
        for g, n in DMA_GROUPS.items():
            toks += [self.tok("%s%d" % (g, i)) for i in range(n)]
        return toks


def _partial_epilogue(nc, P, sb, sbank, dr):
    j = P.join()
    t_ms = P.do("vector", lambda e: e.memset(sb["osb0"][:, :], 0.0),
                waits=j, inc="dve")
    t_st = P.dma("sync", lambda e: e.dma_start(out=dr["out"][0:128, 0:512],
                                               in_=sb["osb0"][:, :]), [t_ms], "st")
    P.do("sync", lambda e: e.nop(), waits=[t_st])


def build_nc(phases=5):
    nc = bass.Bass("TRN2", target_bir_lowering=False)

    dr = dict(
        xkv=nc.dram_tensor("xkv", [S // 2, D], F32, kind="ExternalInput"),
        xq=nc.dram_tensor("xq", [QR, D], F32, kind="ExternalInput"),
        wq=nc.dram_tensor("wq_t", [NH, 128, NKT, 128], F32, kind="ExternalInput"),
        wk=nc.dram_tensor("wk_t", [NH, 128, NKT, 128], F32, kind="ExternalInput"),
        wv=nc.dram_tensor("wv_t", [NH, 128, NKT, 128], F32, kind="ExternalInput"),
        wo=nc.dram_tensor("wo_t", [NH, 128, D], F32, kind="ExternalInput"),
        wg=nc.dram_tensor("wg_t", [64, 128, NKT, 128], F32, kind="ExternalInput"),
        wu=nc.dram_tensor("wu_t", [64, 128, NKT, 128], F32, kind="ExternalInput"),
        wd=nc.dram_tensor("wd_t", [4, 32, 128, 2, 512], F32, kind="ExternalInput"),
        msk=nc.dram_tensor("msk", [128, NQ, MSK_W], BF16, kind="ExternalInput"),
        eye=nc.dram_tensor("eye", [128, 128], BF16, kind="ExternalInput"),
        out=nc.dram_tensor("out", [QR, D], F32, kind="ExternalOutput"),
        x2s=nc.dram_tensor("x2s", [QR, D], F32),
        kvo=nc.dram_tensor("kvo", [NH, 128, 2048], BF16),
        kvi=nc.dram_tensor("kvi", [2, NH, 128, 2048], BF16),
    )

    P = Plan()
    sb = {}
    es = contextlib.ExitStack()
    with es:
        for name, (off, shape, dt) in SB.items():
            sb[name] = nc.alloc_sbuf_tensor_at(name, shape, dt, offset=off)
        psA = es.enter_context(nc.psum_tensor("psA", [128, 1024], F32))
        psB = es.enter_context(nc.psum_tensor("psB", [128, 1024], F32))
        psC = es.enter_context(nc.psum_tensor("psC", [128, 512], F32))
        psD = es.enter_context(nc.psum_tensor("psD", [128, 512], F32))
        psE = es.enter_context(nc.psum_tensor("psE", [128, 512], F32))
        psF = es.enter_context(nc.psum_tensor("psF", [128, 512], F32))
        sbank = {"A0": (psA, 0), "A1": (psA, 512), "B0": (psB, 0),
                 "B1": (psB, 512), "C": (psC, 0), "D": (psD, 0),
                 "E": (psE, 0), "F": (psF, 0)}

        for nm, owner in (("pe", "tensor"), ("dve", "vector"), ("act", "scalar"),
                          ("cc", "gpsimd")):
            P.add_sem(nm, es.enter_context(nc.semaphore(nm)), owner)
        for g, n in DMA_GROUPS.items():
            owner = "gpsimd" if g == "wld" else "sync"
            for i in range(n):
                nm = "%s%d" % (g, i)
                P.add_sem(nm, es.enter_context(nc.semaphore(nm)), owner)

        emit_program(nc, P, sb, sbank, dr, phases)

        with nc.Block() as block:
            for eng in ENGINES:
                fns = P.ops[eng]
                getattr(block, eng)(lambda e, fns=fns: [f(e) for f in fns])
    return nc


def emit_program(nc, P, sb, sbank, dr, phases=5):
    stats = sb["stats"]
    eye = sb["eye"]

    def bk(name, lo=0, hi=512):
        t, base = sbank[name]
        return t[:, base + lo: base + hi]

    t_eye = P.dma("sync", lambda e: e.dma_start(out=sb["eye"][:, :], in_=dr["eye"][:, :]),
                  [], "ld")
    t_msk = P.dma("sync", lambda e: e.dma_start(out=sb["masks"][:, :, :], in_=dr["msk"][:, :, :]),
                  [], "ld")
    t_eps = P.do("vector", lambda e: e.memset(stats[:, C_EPS:C_EPS + 1], 1e-5), inc="dve")
    t_b44 = P.do("vector", lambda e: e.memset(stats[:, C_EPS + 1:C_EPS + 2], -44.0), inc="dve")
    c44 = stats[:, C_EPS + 1:C_EPS + 2]

    def emit_ln(xt_ap, dst_ap, par, pre_waits, trash_ap):
        neg = stats[:, C_NEG + par: C_NEG + par + 1]
        aux = stats[:, C_AUX + par: C_AUX + par + 1]
        aux2 = stats[:, C_AUX2 + par: C_AUX2 + par + 1]
        inv = stats[:, C_INV + par: C_INV + par + 1]
        t_red = P.do("vector", lambda e: e.tensor_reduce(aux, xt_ap, AXX, AOP.add),
                     waits=pre_waits, inc="dve")
        t_nm = P.do("vector", lambda e: e.tensor_scalar(neg, aux, -1.0 / D, None, AOP.mult),
                    waits=[t_red], inc="dve")
        t_sq = P.do("scalar", lambda e: e.activation(trash_ap, xt_ap, AF.Square,
                                                     bias=neg, scale=1.0, accum_out=aux),
                    waits=[t_nm, t_eps], inc="act")
        t_ln = P.do("scalar", lambda e: e.activation(aux2, aux, AF.Ln,
                                                     bias=stats[:, C_EPS:C_EPS + 1], scale=1.0 / D),
                    waits=[t_sq], inc="act")
        t_inv = P.do("scalar", lambda e: e.activation(inv, aux2, AF.Exp, scale=-0.5),
                     waits=[t_ln], inc="act")
        toks = []
        for q in range(4):
            toks.append(P.do("vector", lambda e, q=q: e.tensor_scalar(
                dst_ap[:, q * 512:(q + 1) * 512], xt_ap[:, q * 512:(q + 1) * 512],
                neg, inv, AOP.add, AOP.mult),
                waits=[t_inv, t_nm], inc="dve"))
        return toks

    cd_free = {"C": None, "D": None}
    ef_free = {"E": None, "F": None}

    def emit_transposes(src, n, dst4_fn, wait_fn, eng="scalar"):
        last = None
        for c in range(n):
            sl = c % 4
            bank = "E" if (c // 4) % 2 == 0 else "F"
            w = wait_fn(c) + [t_eye]
            if sl == 0:
                w.append(ef_free[bank])
            t_mm = P.do("tensor", lambda e, c=c, sl=sl, bank=bank: e.matmul(
                bk(bank, sl * 128, (sl + 1) * 128), src[:, c * 128:(c + 1) * 128],
                eye[:, :], start=True, stop=True),
                waits=w, inc="pe" if sl == 3 else None)
            if sl == 3:
                if eng == "scalar":
                    t_ev = P.do("scalar", lambda e, c=c, bank=bank: e.copy(
                        dst4_fn(c - 3), bk(bank, 0, 512)), waits=[t_mm], inc="act")
                else:
                    t_ev = P.do("vector", lambda e, c=c, bank=bank: e.tensor_copy(
                        dst4_fn(c - 3), bk(bank, 0, 512)), waits=[t_mm], inc="dve")
                ef_free[bank] = t_ev
                last = t_mm
        return last

    xt = [sb["xt0"], sb["xt1"], sb["xt2"], sb["xt3"]]
    htm = [sb["htm0"], sb["htm1"], sb["htm2"], sb["htm3"]]
    xt_free = [None] * 4
    htm_free = [None] * 4
    for qt in range(NQ):
        par = qt % 4
        t_ld = P.dma("sync", lambda e, qt=qt, par=par: e.dma_start(
            out=xt[par][:, :], in_=dr["xq"][qt * 128:(qt + 1) * 128, :]),
            [xt_free[par]], "ld")
        t_norm = emit_ln(xt[par][:, :], htm[par][:, :], par,
                         [t_ld, htm_free[par]], sb["trash"][:, :])
        xt_free[par] = t_norm
        htm_free[par] = emit_transposes(
            htm[par], NKT,
            lambda c0, qt=qt: sb["h1q"][:, c0:c0 + 4, qt * 128:(qt + 1) * 128],
            lambda c: [t_norm[c // 4]])

    t_h1q_done = P.tok("act")

    ab_free = {"A0": None, "A1": None, "B0": None, "B1": None}
    qbuf_free = None

    def emit_q_head(h):
        nonlocal qbuf_free
        t_w = P.dma("gpsimd", lambda e, h=h: e.dma_start(
            out=sb["wqbuf"][:, :, :], in_=dr["wq"][h, :, :, :]),
            [qbuf_free], "wld")
        banks = ("A0", "A1") if (h % 2 == 0) else ("B0", "B1")
        for sp in range(2):
            bank = banks[sp]
            t_mm = None
            for c in range(NKT):
                w = [t_w, t_h1q_done, ab_free[bank]] if c == 0 else []
                t_mm = P.do("tensor", lambda e, h=h, sp=sp, c=c, bank=bank: e.matmul(
                    bk(bank), sb["wqbuf"][:, c, :],
                    sb["h1q"][:, c, sp * 512:(sp + 1) * 512],
                    start=(c == 0), stop=(c == NKT - 1)), waits=w,
                    inc="pe" if c == NKT - 1 else None)
            t_ev = P.do("scalar", lambda e, h=h, sp=sp, bank=bank: e.mul(
                sb["q_all"][:, h, sp * 512:(sp + 1) * 512], bk(bank), ISQ),
                waits=[t_mm], inc="act")
            ab_free[bank] = t_ev
        qbuf_free = P.tok("pe")

    for i in range(8):
        par = i % 4
        t_ld = P.dma("sync", lambda e, i=i, par=par: e.dma_start(
            out=xt[par][:, :], in_=dr["xkv"][i * 128:(i + 1) * 128, :]),
            [xt_free[par]], "ld")
        t_norm = emit_ln(xt[par][:, :], htm[par][:, :], par,
                         [t_ld, htm_free[par]], sb["trash"][:, :])
        xt_free[par] = t_norm
        htm_free[par] = emit_transposes(
            htm[par], NKT,
            lambda c0, i=i: sb["h1_fm"][:, c0:c0 + 4, i * 128:(i + 1) * 128],
            lambda c: [t_norm[c // 4]])
        emit_q_head(2 * i)
        emit_q_head(2 * i + 1)

    t_h1_done = P.tok("act")
    if phases <= 1:
        return _partial_epilogue(nc, P, sb, sbank, dr)

    kfm = [sb["kfm0"], sb["kfm1"]]
    vtm = [sb["vtm0"], sb["vtm1"]]
    pt = [sb["pt0"], sb["pt1"], sb["pt2"], sb["pt3"]]
    wk_dma_tok = [None] * NH
    wv_dma_tok = [None] * NH
    kv_ev = [None] * 4
    k_in_tok = [[None] * 2, [None] * 2]
    v_in_tok = [[None] * 2, [None] * 2]
    kfm_free = [None, None]
    stg_free = None
    kbuf_free = None
    vbuf_free = None
    vfm_free = None
    pt_free = [None, None, None, None]
    grp_free = [None, None]
    grp_exp = [[], []]
    p_region_free = [None] * NQ
    diag_free = [None] * NQ
    exp_tok = [None] * NQ
    diag_tok = [None] * NQ

    def emit_wk_dma(h):
        wk_dma_tok[h] = P.dma("gpsimd", lambda e, h=h: e.dma_start(
            out=sb["wkbuf"][:, :, :], in_=dr["wk"][h, :, :, :]),
            [kbuf_free], "wld")

    def emit_wv_dma(h):
        wv_dma_tok[h] = P.dma("gpsimd", lambda e, h=h: e.dma_start(
            out=sb["wvbuf"][:, :, :], in_=dr["wv"][h, :, :, :]),
            [vbuf_free], "wld")

    def emit_k(h):
        nonlocal kbuf_free
        for sp in range(2):
            bank = SBANKS[sp]
            t_mm = None
            for c in range(NKT):
                w = [wk_dma_tok[h], t_h1_done, ab_free[bank]] if c == 0 else []
                t_mm = P.do("tensor", lambda e, h=h, sp=sp, c=c, bank=bank: e.matmul(
                    bk(bank), sb["wkbuf"][:, c, :],
                    sb["h1_fm"][:, c, sp * 512:(sp + 1) * 512],
                    start=(c == 0), stop=(c == NKT - 1)), waits=w,
                    inc="pe" if c == NKT - 1 else None)
            t_ev = P.do("vector", lambda e, sp=sp, bank=bank: e.tensor_copy(
                sb["kvstg"][:, sp * 512:(sp + 1) * 512], bk(bank)),
                waits=[t_mm, stg_free], inc="dve")
            kv_ev[sp] = t_ev
            ab_free[bank] = t_ev
        kbuf_free = P.tok("pe")

    def emit_v(h):
        nonlocal vbuf_free, stg_free
        for sp in range(2):
            bank = SBANKS[2 + sp]
            t_mm = None
            for c in range(NKT):
                w = [wv_dma_tok[h], ab_free[bank]] if c == 0 else []
                t_mm = P.do("tensor", lambda e, h=h, sp=sp, c=c, bank=bank: e.matmul(
                    bk(bank), sb["wvbuf"][:, c, :],
                    sb["h1_fm"][:, c, sp * 512:(sp + 1) * 512],
                    start=(c == 0), stop=(c == NKT - 1)), waits=w,
                    inc="pe" if c == NKT - 1 else None)
            t_ev = P.do("vector", lambda e, sp=sp, bank=bank: e.tensor_copy(
                sb["kvstg"][:, 1024 + sp * 512:1024 + (sp + 1) * 512], bk(bank)),
                waits=[t_mm, stg_free], inc="dve")
            kv_ev[2 + sp] = t_ev
            ab_free[bank] = t_ev
        vbuf_free = P.tok("pe")
        stg_free = P.dma("sync", lambda e, h=h: e.dma_start(
            out=dr["kvo"][h, :, :], in_=sb["kvstg"][:, :]), list(kv_ev), "st")

    def emit_kv_in(h):
        par = h % 2
        for r in range(2):
            k_in_tok[par][r] = P.dma("sync", lambda e, r=r, par=par, h=h: e.dma_start(
                out=kfm[par][:, r * 1024:(r + 1) * 1024], in_=dr["kvi"][r, h, :, 0:1024]),
                [t_cc, kfm_free[par]], "ld")
            v_in_tok[par][r] = P.dma("sync", lambda e, r=r, par=par, h=h: e.dma_start(
                out=sb["vfm"][:, r * 1024:(r + 1) * 1024], in_=dr["kvi"][r, h, :, 1024:2048]),
                [t_cc, vfm_free], "ld")

    def emit_vt(h):
        nonlocal vfm_free
        vfm_free = emit_transposes(
            sb["vfm"], NKT,
            lambda j0, h=h: vtm[h % 2][:, j0:j0 + 4, :],
            lambda j, h=h: [v_in_tok[h % 2][j // 8]], eng="vector")

    def emit_scores_softmax(h, s):
        K = KLENS[s]
        sfx = K - MSK_W
        g = s // 4
        nch = 4 if g == 0 else 2
        sstat = sb["sstat"]
        mm_toks = []
        for c in range(nch):
            lo = c * 512
            hi = min(K, lo + 512)
            bank = SBANKS[c]
            w = [ab_free[bank], t_msk, t_eye, k_in_tok[h % 2][c // 2]]
            masked = hi > sfx
            t_mm = P.do("tensor", lambda e, h=h, s=s, lo=lo, hi=hi, bank=bank, masked=masked: e.matmul(
                bk(bank, 0, hi - lo), sb["q_all"][:, h, s * 128:(s + 1) * 128],
                kfm[h % 2][:, lo:hi], start=True, stop=not masked),
                waits=w, inc=None if masked else "pe")
            if masked:
                mlo = max(lo, sfx)
                t_mm = P.do("tensor", lambda e, s=s, mlo=mlo, hi=hi, lo=lo, sfx=sfx, bank=bank: e.matmul(
                    bk(bank, mlo - lo, hi - lo), eye[:, :],
                    sb["masks"][:, s, mlo - sfx:hi - sfx],
                    start=False, stop=True), inc="pe")
            mm_toks.append(t_mm)
        nsec = (nch + 1) // 2
        for sec in range(nsec):
            slo = sec * 1024
            shi = min(K, slo + 1024)
            sc_t = sbank[SBANKS[2 * sec]][0]
            t_exp = P.do("scalar", lambda e, s=s, slo=slo, shi=shi, sec=sec, sc_t=sc_t, sstat=sstat: e.activation(
                sb["p_sb"][:, P_OFF[s] + slo:P_OFF[s] + shi], sc_t[:, 0:shi - slo],
                AF.Exp, bias=c44, scale=1.0,
                accum_out=sstat[:, 48 + s * 2 + sec:49 + s * 2 + sec]),
                waits=[mm_toks[2 * sec], mm_toks[2 * sec + 1], t_b44,
                       p_region_free[s], grp_free[g]], inc="act")
            grp_exp[g].append(t_exp)
            ab_free[SBANKS[2 * sec]] = t_exp
            ab_free[SBANKS[2 * sec + 1]] = t_exp
        exp_tok[s] = grp_exp[g][-1]

    def emit_combine(h, g):
        sstat = sb["sstat"]
        st_t = sb["stats"]
        if g == 0:
            sums3 = bass.AP(sstat, 48, [[96, 128], [2, 4], [1, 2]])
            sgc = st_t[:, C_SG2:C_SG2 + 4]
            rgc = st_t[:, C_RG:C_RG + 4]
            t_sg = P.do("vector", lambda e, sums3=sums3, sgc=sgc: e.tensor_reduce(
                sgc, sums3, AXX, AOP.add), waits=grp_exp[g] + [grp_free[g]], inc="dve")
            t_rc = P.do("vector", lambda e, sgc=sgc, rgc=rgc: e.reciprocal(rgc, sgc),
                        waits=[t_sg], inc="dve")
        else:
            ssums = bass.AP(sstat, 56, [[96, 128], [2, 4]])
            rgc = st_t[:, C_RG + 4:C_RG + 8]
            t_rc = P.do("vector", lambda e, ssums=ssums, rgc=rgc: e.reciprocal(rgc, ssums),
                        waits=grp_exp[g] + [grp_free[g]], inc="dve")
        dtok = None
        for si in range(4):
            s = 4 * g + si
            dtok = P.do("vector", lambda e, s=s, si=si, rgc=rgc: e.tensor_scalar(
                sb["diag"][:, s, 0, :], eye[:, :], rgc[:, si:si + 1], None, AOP.mult),
                waits=[t_rc, t_eye, diag_free[s]], inc="dve")
            diag_tok[s] = dtok
        grp_free[g] = dtok
        grp_exp[g] = []

    def tpv_steps(h, g, slots):
        obank = "E" if g == 0 else "F"
        maxJ = JS[slots[0]]
        pend_pv = []
        steps = []

        def step_j(j):
            valid = [s for s in slots if JS[s] > j]
            nv = len(valid)
            jpar = j % 4
            bank = "C" if j % 2 == 0 else "D"

            t_pt = None
            for sl, s in enumerate(valid):
                w = [diag_tok[s], exp_tok[s]]
                if sl == 0:
                    w.append(cd_free[bank])
                t_pt = P.do("tensor", lambda e, s=s, j=j, sl=sl, bank=bank: e.matmul(
                    bk(bank, sl * 128, (sl + 1) * 128),
                    sb["p_sb"][:, P_OFF[s] + j * 128:P_OFF[s] + (j + 1) * 128],
                    sb["diag"][:, s, 0, :], start=True, stop=True), waits=w,
                    inc="pe" if sl == nv - 1 else None)
            for s in valid:
                p_region_free[s] = t_pt
                diag_free[s] = t_pt
            if j % 2 == 0:
                t_ev = P.do("scalar", lambda e, jpar=jpar, nv=nv, bank=bank: e.copy(
                    pt[jpar][:, 0:nv * 128], bk(bank, 0, nv * 128)),
                    waits=[t_pt, pt_free[jpar]], inc="act")
            else:
                t_ev = P.do("vector", lambda e, jpar=jpar, nv=nv, bank=bank: e.tensor_copy(
                    pt[jpar][:, 0:nv * 128], bk(bank, 0, nv * 128)),
                    waits=[t_pt, pt_free[jpar]], inc="dve")
            cd_free[bank] = t_ev
            w = [t_ev]
            if j == 0:
                w.append(ef_free[obank])

            def _pv(j=j, nv=nv, jpar=jpar, w=tuple(w)):
                t_pv = P.do("tensor", lambda e, h=h, j=j, nv=nv, jpar=jpar, obank=obank, maxJ=maxJ: e.matmul(
                    bk(obank, 0, nv * 128), vtm[h % 2][:, j, :],
                    pt[jpar][:, 0:nv * 128], start=(j == 0), stop=(j == maxJ - 1)),
                    waits=list(w), inc="pe")
                pt_free[jpar] = t_pv
            if len(pend_pv) >= 2:
                pend_pv.pop(0)()
            pend_pv.append(_pv)

        def tail():
            while pend_pv:
                pend_pv.pop(0)()
            t_oev = P.do("vector", lambda e, h=h, g=g, obank=obank: e.tensor_copy(
                sb["o_fm"][:, h, g * 512:(g + 1) * 512], bk(obank)),
                waits=[P.tok("pe")], inc="dve")
            ef_free[obank] = t_oev

        for j in range(maxJ):
            steps.append(lambda j=j: step_j(j))
        steps.append(tail)
        return steps

    for h in range(NH):
        emit_wk_dma(h)
        emit_k(h)
        emit_wv_dma(h)
        emit_v(h)
    t_cc = P.do("gpsimd", lambda e: e.collective_compute(
        "AllGather", AOP.bypass,
        replica_groups=[[0, 1], [2, 3], [4, 5], [6, 7]],
        ins=[dr["kvo"][:, :, :]], outs=[dr["kvi"][:, :, :, :]]),
        waits=[stg_free, P.tok("st0"), P.tok("st1")], inc="cc")
    emit_kv_in(0)
    pend = []

    def drain(k):
        for _ in range(min(k, len(pend))):
            pend.pop(0)()

    for h in range(NH):
        for i, s in enumerate((0, 1, 2, 3)):
            emit_scores_softmax(h, s)
            drain(3)
        emit_combine(h, 0)
        drain(99)
        emit_vt(h)
        pend = tpv_steps(h, 0, [0, 1, 2, 3])
        for i, s in enumerate((4, 5, 6, 7)):
            emit_scores_softmax(h, s)
            drain(5)
        emit_combine(h, 1)
        kfm_free[h % 2] = P.tok("pe")
        if h + 1 < NH:
            emit_kv_in(h + 1)
        drain(99)
        pend = tpv_steps(h, 1, [4, 5, 6, 7])
    drain(99)

    j_p2 = P.join()
    if phases <= 2:
        return _partial_epilogue(nc, P, sb, sbank, dr)

    wo_dma_tok = [None] * NH
    for of in range(NH):
        wo_dma_tok[of] = P.dma("gpsimd", lambda e, of=of: e.dma_start(
            out=sb["woT"][:, of, :], in_=dr["wo"][of, :, :]),
            j_p2 if of == 0 else [], "wld")

    xot = [sb["xot0"], sb["xot1"]]
    x2t = [sb["x2t0"], sb["x2t1"]]
    h2tm = [sb["h2tm0"], sb["h2tm1"]]
    xot_free = [None, None]
    x2t_free = [[], []]
    h2tm_free = [None, None]
    h2T_done = [None, None]
    wo_banks = ["A0", "A1", "B0", "B1", "C", "D", "E", "F"]
    wo_bank_free = {b: None for b in wo_banks}
    x2st_tok = [None] * NQ
    pend_T = []

    def emit_h2T(slot, par):
        for c in range(NKT):
            bank = "C" if (c % 2 == 0) else "D"
            t_mm = P.do("tensor", lambda e, c=c, par=par, bank=bank: e.matmul(
                bk(bank, 0, 128), h2tm[par][:, c * 128:(c + 1) * 128],
                eye[:, :], start=True, stop=True),
                waits=[h2tm_free[par], wo_bank_free[bank]], inc="pe")
            t_ev = P.do("scalar", lambda e, c=c, slot=slot, bank=bank: e.copy(
                sb["h2_fm"][:, c, slot * 128:(slot + 1) * 128],
                bk(bank, 0, 128)), waits=[t_mm], inc="act")
            wo_bank_free[bank] = t_ev
        h2T_done[par] = P.tok("pe")

    def emit_wo_slot(slot, idx, first):
        banks = wo_banks[idx * 4:(idx + 1) * 4]
        t_mm = None
        for of in range(NH):
            for q in range(4):
                bank = banks[q]
                w = [wo_dma_tok[of]] if q == 0 else []
                if of == 0:
                    w = w + [wo_bank_free[bank]] + (j_p2 if first else [])
                t_mm = P.do("tensor", lambda e, of=of, q=q, slot=slot, bank=bank: e.matmul(
                    bk(bank), sb["o_fm"][:, of, slot * 128:(slot + 1) * 128],
                    sb["woT"][:, of, q * 512:(q + 1) * 512],
                    start=(of == 0), stop=(of == NH - 1)), waits=w,
                    inc="pe" if (of == NH - 1 and q == 3) else None)
        return t_mm, banks

    def emit_wo_epilogue(slot, banks):
        par = slot % 2
        ev_toks = []
        for q in range(4):
            bank = banks[q]
            t_ev = P.do("vector", lambda e, q=q, par=par, bank=bank: e.tensor_add(
                x2t[par][:, q * 512:(q + 1) * 512], bk(bank),
                xot[par][:, q * 512:(q + 1) * 512]),
                waits=[t_wo_mm[slot], xot_free[par]] + x2t_free[par], inc="dve")
            wo_bank_free[bank] = t_ev
            ev_toks.append(t_ev)
        xot_free[par] = ev_toks[-1]
        x2st_tok[slot] = P.dma("sync", lambda e, slot=slot, par=par: e.dma_start(
            out=dr["x2s"][slot * 128:(slot + 1) * 128, :], in_=x2t[par][:, :]),
            ev_toks, "st")
        t_norm = emit_ln(x2t[par][:, :], h2tm[par][:, :], par,
                         ev_toks + [h2T_done[par]], sb["trash3"][:, :])
        x2t_free[par] = t_norm + [x2st_tok[slot]]
        h2tm_free[par] = t_norm
        pend_T.append((slot, par))

    t_wo_mm = {}
    for r in range(4):
        slots = (2 * r, 2 * r + 1)
        for slot in slots:
            par = slot % 2
            xot_free[par] = P.dma("sync", lambda e, slot=slot, par=par: e.dma_start(
                out=xot[par][:, :], in_=dr["xq"][slot * 128:(slot + 1) * 128, :]),
                [xot_free[par]] + (j_p2 if r == 0 else []), "ld")
        t_wo_mm[slots[0]], banks0 = emit_wo_slot(slots[0], 0, r == 0)
        if r > 0:
            for slot, par in pend_T[:2]:
                emit_h2T(slot, par)
            del pend_T[:2]
        emit_wo_epilogue(slots[0], banks0)
        t_wo_mm[slots[1]], banks1 = emit_wo_slot(slots[1], 1, False)
        emit_wo_epilogue(slots[1], banks1)
    for slot, par in pend_T:
        emit_h2T(slot, par)
    pend_T.clear()

    j_p3 = P.join()
    if phases <= 3:
        return _partial_epilogue(nc, P, sb, sbank, dr)

    wg = [sb["wg0"], sb["wg1"]]
    wu = [sb["wu0"], sb["wu1"]]
    wg_dma = [None] * 64
    wu_dma = [None] * 64
    wg_free = [None, None]
    wu_free = [None, None]
    gate_banks = [("A0", "A1"), ("C", "D")]
    up_banks = [("B0", "B1"), ("E", "F")]
    gbank_free = {b: None for b in wo_banks}
    hid_tok = [None] * 64
    silu_free = [None, None]

    def emit_wgu_dma(hc):
        par = hc % 2
        wg_dma[hc] = P.dma("gpsimd", lambda e, hc=hc, par=par: e.dma_start(
            out=wg[par][:, :, :], in_=dr["wg"][hc, :, :, :]),
            [wg_free[par]] + (j_p3 if hc < 2 else []), "wld")
        wu_dma[hc] = P.dma("gpsimd", lambda e, hc=hc, par=par: e.dma_start(
            out=wu[par][:, :, :], in_=dr["wu"][hc, :, :, :]),
            [wu_free[par]], "wld")

    emit_wgu_dma(0)
    emit_wgu_dma(1)
    for hc in range(64):
        par = hc % 2
        gb = gate_banks[par]
        ub = up_banks[par]
        t_gmm = None
        for c in range(NKT):
            for half in range(2):
                w = []
                if c == 0 and half == 0:
                    w = [wg_dma[hc], j_p3, gbank_free[gb[0]], gbank_free[gb[1]]]
                t_gmm = P.do("tensor", lambda e, hc=hc, c=c, half=half, par=par, gb=gb: e.matmul(
                    bk(gb[half]), wg[par][:, c, :],
                    sb["h2_fm"][:, c, half * 512:(half + 1) * 512],
                    start=(c == 0), stop=(c == NKT - 1)), waits=w,
                    inc="pe" if (c == NKT - 1 and half == 1) else None)
        t_umm = None
        for c in range(NKT):
            for half in range(2):
                w = []
                if c == 0 and half == 0:
                    w = [wu_dma[hc], gbank_free[ub[0]], gbank_free[ub[1]]]
                t_umm = P.do("tensor", lambda e, hc=hc, c=c, half=half, par=par, ub=ub: e.matmul(
                    bk(ub[half]), wu[par][:, c, :],
                    sb["h2_fm"][:, c, half * 512:(half + 1) * 512],
                    start=(c == 0), stop=(c == NKT - 1)), waits=w,
                    inc="pe" if (c == NKT - 1 and half == 1) else None)
        wg_free[par] = t_gmm
        wu_free[par] = t_umm
        if hc + 2 < 64:
            emit_wgu_dma(hc + 2)
        t_silu = [None, None]
        for half in range(2):
            t_silu[half] = P.do("scalar", lambda e, half=half, gb=gb: e.activation(
                sb["silu"][:, half * 512:(half + 1) * 512],
                bk(gb[half]), AF.Silu),
                waits=[t_gmm, silu_free[half]], inc="act")
            gbank_free[gb[half]] = t_silu[half]
        muls = []
        for half in range(2):
            t_mul = P.do("vector", lambda e, hc=hc, half=half, ub=ub: e.tensor_tensor(
                sb["hidden"][:, hc, half * 512:(half + 1) * 512],
                sb["silu"][:, half * 512:(half + 1) * 512],
                bk(ub[half]), AOP.mult),
                waits=[t_umm, t_silu[half], j_p3], inc="dve")
            gbank_free[ub[half]] = t_mul
            silu_free[half] = t_mul
            muls.append(t_mul)
        hid_tok[hc] = muls[-1]

    j_p4 = P.join()
    if phases <= 4:
        return _partial_epilogue(nc, P, sb, sbank, dr)

    wd = [sb["wd0"], sb["wd1"], sb["wd2"], sb["wd3"]]
    x2r = [sb["x2r0"], sb["x2r1"]]
    osb = [sb["osb0"], sb["osb1"]]
    wd_free = [None, None, None, None]
    x2r_free = [None, None]
    osb_free = [None, None]
    dbank_free = {b: None for b in wo_banks}
    wd_dma = {}
    x2r_dma = {}

    def emit_wd_dma(qtr, pair):
        par = pair % 4
        wd_dma[(qtr, pair)] = P.dma("gpsimd", lambda e, qtr=qtr, pair=pair, par=par: e.dma_start(
            out=wd[par][:, :, :], in_=dr["wd"][qtr, pair, :, :, :]),
            [wd_free[par]], "wld")

    def emit_x2r_dma(qtr, tt):
        par = tt % 2
        x2r_dma[(qtr, tt)] = P.dma("sync", lambda e, qtr=qtr, tt=tt, par=par: e.dma_start(
            out=x2r[par][:, :], in_=dr["x2s"][tt * 128:(tt + 1) * 128,
                                              qtr * 512:(qtr + 1) * 512]),
            [x2r_free[par], x2st_tok[tt]], "ld")

    for k in range(4):
        emit_wd_dma(0, k)
    emit_x2r_dma(0, 0)
    emit_x2r_dma(0, 1)
    for qtr in range(4):
        for hc in range(64):
            par = (hc // 2) % 4
            t_mm = None
            for tt in range(8):
                bank = wo_banks[tt]
                w = []
                if tt == 0:
                    w.append(wd_dma[(qtr, hc // 2)])
                    if qtr == 0:
                        w.append(hid_tok[hc])
                        if hc == 0:
                            w.extend(j_p4)
                if hc == 0:
                    w.append(dbank_free[bank])
                t_mm = P.do("tensor", lambda e, qtr=qtr, hc=hc, tt=tt, par=par, bank=bank: e.matmul(
                    bk(bank), sb["hidden"][:, hc, tt * 128:(tt + 1) * 128],
                    wd[par][:, hc % 2, :], start=(hc == 0), stop=(hc == 63)), waits=w,
                    inc="pe" if tt == 7 else None)
            if hc % 2 == 1:
                wd_free[par] = t_mm
                pair = hc // 2
                nxt = (qtr, pair + 4) if pair + 4 < 32 else (qtr + 1, pair + 4 - 32)
                if nxt[0] < 4:
                    emit_wd_dma(*nxt)
        t_mm_all = P.tok("pe")
        for tt in range(8):
            bank = wo_banks[tt]
            par = tt % 2
            t_ev = P.do("vector", lambda e, tt=tt, par=par, bank=bank: e.tensor_add(
                osb[par][:, :], bk(bank), x2r[par][:, :]),
                waits=[t_mm_all, x2r_dma[(qtr, tt)], osb_free[par]], inc="dve")
            dbank_free[bank] = t_ev
            x2r_free[par] = t_ev
            t_st = P.dma("sync", lambda e, qtr=qtr, tt=tt, par=par: e.dma_start(
                out=dr["out"][tt * 128:(tt + 1) * 128, qtr * 512:(qtr + 1) * 512],
                in_=osb[par][:, :]), [t_ev], "st")
            osb_free[par] = t_st
            nq = (qtr, tt + 2) if tt + 2 < 8 else (qtr + 1, tt + 2 - 8)
            if nq[0] < 4:
                emit_x2r_dma(*nq)

    P.do("sync", lambda e: e.nop(),
         waits=[P.tok("st%d" % i) for i in range(DMA_GROUPS["st"])])


_NC_CACHE = {}


def _get_nc():
    if "nc" not in _NC_CACHE:
        _NC_CACHE["nc"] = build_nc()
    return _NC_CACHE["nc"]


def _host_prep(wq, wk, wv, wo, wg, wu, wd):
    def qkv_tile(w):
        return np.ascontiguousarray(
            w.reshape(NH, 128, NKT, 128).transpose(0, 3, 2, 1))

    wq_t = qkv_tile(wq)
    wk_t = qkv_tile(wk)
    wv_t = qkv_tile(wv)
    wo_t = np.ascontiguousarray(wo.reshape(D, NH, 128).transpose(1, 2, 0))
    wg_t = np.ascontiguousarray(wg.reshape(64, 128, NKT, 128).transpose(0, 3, 2, 1))
    wu_t = np.ascontiguousarray(wu.reshape(64, 128, NKT, 128).transpose(0, 3, 2, 1))
    wd_t = np.ascontiguousarray(
        wd.reshape(4, 512, 32, 2, 128).transpose(0, 2, 4, 3, 1))
    return wq_t, wk_t, wv_t, wo_t, wg_t, wu_t, wd_t


def _masks_for_class(cls):
    m = np.zeros((128, NQ, MSK_W), np.float32)
    for s in range(NQ):
        t = SLOT_TILES[cls][s]
        K = KLENS[s]
        sfx = K - MSK_W
        for jj in range(MSK_W):
            col = sfx + jj
            bad = col - t * 128
            if bad > 0:
                m[:min(bad, 128), s, jj] = -1e9
    return m.astype(BF)


def kernel(L0_attn_x, L0_attn_wq, L0_attn_wk, L0_attn_wv, L0_attn_wo,
           L0_ffn_w_gate, L0_ffn_w_up, L0_ffn_w_down):
    x = np.ascontiguousarray(np.asarray(L0_attn_x, dtype=np.float32))
    wq = np.asarray(L0_attn_wq, dtype=np.float32)
    wk = np.asarray(L0_attn_wk, dtype=np.float32)
    wv = np.asarray(L0_attn_wv, dtype=np.float32)
    wo = np.asarray(L0_attn_wo, dtype=np.float32)
    wg = np.asarray(L0_ffn_w_gate, dtype=np.float32)
    wu = np.asarray(L0_ffn_w_up, dtype=np.float32)
    wd = np.asarray(L0_ffn_w_down, dtype=np.float32)

    wq_t, wk_t, wv_t, wo_t, wg_t, wu_t, wd_t = _host_prep(wq, wk, wv, wo, wg, wu, wd)
    eye = np.eye(128, dtype=np.float32).astype(BF)
    masks = [_masks_for_class(0), _masks_for_class(1)]
    perms = []
    for cls in range(2):
        perms.append(np.concatenate(
            [np.arange(t * 128, (t + 1) * 128) for t in SLOT_TILES[cls]]))

    nc = _get_nc()
    in_maps = []
    for c in range(8):
        b, cls = c // 2, c % 2
        in_maps.append({
            "xkv": np.ascontiguousarray(x[b][(c % 2) * 1024:(c % 2) * 1024 + 1024]),
            "xq": np.ascontiguousarray(x[b][perms[cls]]),
            "wq_t": wq_t, "wk_t": wk_t, "wv_t": wv_t, "wo_t": wo_t,
            "wg_t": wg_t, "wu_t": wu_t, "wd_t": wd_t,
            "msk": masks[cls], "eye": eye,
        })
    res = run_bass_kernel_spmd(nc, in_maps, core_ids=list(range(8)))
    out = np.empty((B, S, D), np.float32)
    for c in range(8):
        b, cls = c // 2, c % 2
        out[b][perms[cls]] = res.results[c]["out"]
    return out


# version 20
# speedup vs baseline: 1.0387x; 1.0387x over previous
import numpy as np
import ml_dtypes
import contextlib
import concourse.bass as bass
import concourse.mybir as mybir
from concourse.bass_utils import run_bass_kernel_spmd

F32 = mybir.dt.float32
BF16 = mybir.dt.bfloat16
AOP = mybir.AluOpType
AF = mybir.ActivationFunctionType
AXX = mybir.AxisListType.X
BF = ml_dtypes.bfloat16

D = 2048
S = 2048
B = 4
NH = 16
HD = 128
FF = 8192
NQ = 8
QR = NQ * 128
NKT = S // 128
KLENS = [2048, 1920, 1792, 1664, 1024, 896, 768, 640]
JS = [k // 128 for k in KLENS]
SLOT_TILES = [[15, 14, 13, 12, 3, 2, 1, 0],
              [11, 10, 9, 8, 7, 6, 5, 4]]
MSK_W = 640
P_OFF = [0]
for k in KLENS:
    P_OFF.append(P_OFF[-1] + k)
P_TOT = P_OFF[-1]
ISQ = float(1.0 / np.sqrt(HD))
SBANKS = ["A0", "A1", "B0", "B1"]

SB = {
    "eye":    (16512,      [128, 128], BF16),
    "stats":  (16768,    [128, 64],  F32),
    "h1_fm":  (17024,   [128, NKT, S // 2], BF16),
    "kvstg":  (49792,  [128, 2048], BF16),
    "q_all":  (82560,  [128, NH, QR], BF16),
    "h1q":    (115328,  [128, NKT, QR], BF16),
    "xt0":    (148096, [128, D], F32),
    "xt1":    (156288, [128, D], F32),
    "xt2":    (164480, [128, D], F32),
    "xt3":    (172672, [128, D], F32),
    "htm0":   (180864, [128, D], BF16),
    "htm1":   (184960, [128, D], BF16),
    "htm2":   (189056, [128, D], BF16),
    "htm3":   (193152, [128, D], BF16),
    "trash":  (197248, [128, D], BF16),
    "wqbuf":  (201344, [128, NKT, 128], BF16),
    "o_fm":   (115328,  [128, NH, QR], BF16),
    "kfm0":   (148096, [128, S], BF16),
    "kfm1":   (152192, [128, S], BF16),
    "vfm":    (156288, [128, S], BF16),
    "vtm0":   (160384, [128, NKT, 128], BF16),
    "vtm1":   (164480, [128, NKT, 128], BF16),
    "p_sb":   (184960, [128, P_TOT], BF16),
    "masks":  (206464, [128, NQ, MSK_W], BF16),
    "diag":   (168576, [128, NQ, 4, 128], BF16),
    "sstat":  (176768, [128, 96], F32),
    "pt0":    (218752, [128, 512], BF16),
    "pt1":    (219776, [128, 512], BF16),
    "pt2":    (177152, [128, 512], BF16),
    "pt3":    (178176, [128, 512], BF16),
    "wkbuf":  (220800, [128, NKT, 128], BF16),
    "wvbuf":  (224896, [128, NKT, 128], BF16),
    "h2_fm":  (17024,   [128, NKT, QR], BF16),
    "woT":    (49792,  [128, NH, D], BF16),
    "xot0":   (148096, [128, D], F32),
    "xot1":   (156288, [128, D], F32),
    "x2t0":   (164480, [128, D], F32),
    "x2t1":   (172672, [128, D], F32),
    "h2tm0":  (180864, [128, D], BF16),
    "h2tm1":  (184960, [128, D], BF16),
    "trash3": (189056, [128, D], BF16),
    "hidden": (49792,  [128, 64, QR], BF16),
    "wg0":    (180864, [128, NKT, 128], BF16),
    "wg1":    (184960, [128, NKT, 128], BF16),
    "wu0":    (189056, [128, NKT, 128], BF16),
    "wu1":    (193152, [128, NKT, 128], BF16),
    "silu":   (197248, [128, QR], BF16),
    "wd0":    (199296, [128, 2, 512], BF16),
    "wd1":    (209536, [128, 2, 512], BF16),
    "wd2":    (211584, [128, 2, 512], BF16),
    "wd3":    (213632, [128, 2, 512], BF16),
    "x2r0":   (201344, [128, 512], F32),
    "x2r1":   (203392, [128, 512], F32),
    "osb0":   (205440, [128, 512], F32),
    "osb1":   (207488, [128, 512], F32),
}
C_NEG, C_AUX, C_AUX2, C_INV = 0, 4, 8, 12
C_EPS = 16
C_GM = 20
C_SG2 = 28
C_RG = 36


ENGINES = ("tensor", "vector", "scalar", "sync", "gpsimd")


def _flat(waits):
    out = []
    for t in waits:
        if t is None:
            continue
        if isinstance(t, list):
            out.extend(x for x in t if x is not None)
        else:
            out.append(t)
    return out


DMA_GROUPS = {"ld": 4, "wld": 4, "st": 2}


class Plan:
    def __init__(self):
        self.ops = {e: [] for e in ENGINES}
        self.sems = {}
        self.cnt = {}
        self.owner = {}
        self.waited = {e: {} for e in ENGINES}
        self.rr = {g: 0 for g in DMA_GROUPS}

    def add_sem(self, name, handle, owner):
        self.sems[name] = handle
        self.cnt[name] = 0
        self.owner[name] = owner

    def dma(self, eng, build, waits, group):
        n = DMA_GROUPS[group]
        i = self.rr[group]
        self.rr[group] = (i + 1) % n
        s = "%s%d" % (group, i)
        pre = self.cnt[s]
        return self.do(eng, build, list(waits) + [(s, pre)], inc=s, k=16)

    def wait(self, eng, tok):
        sem, v = tok
        if v <= 0:
            return
        if self.waited[eng].get(sem, -1) >= v:
            return
        h = self.sems[sem]
        self.ops[eng].append(lambda e, h=h, v=v: e.wait_ge(h, v))
        self.waited[eng][sem] = v

    def do(self, eng, build, waits=(), inc=None, k=1):
        for t in _flat(waits):
            self.wait(eng, t)
        if inc is not None:
            self.cnt[inc] += k
            h = self.sems[inc]
            self.ops[eng].append(lambda e, b=build, h=h, kk=k: b(e).then_inc(h, kk))
            return (inc, self.cnt[inc])
        self.ops[eng].append(lambda e, b=build: b(e))
        return None

    def tok(self, sem):
        return (sem, self.cnt[sem])

    def join(self):
        toks = [self.tok(s) for s in ("pe", "dve", "act")]
        for g, n in DMA_GROUPS.items():
            toks += [self.tok("%s%d" % (g, i)) for i in range(n)]
        return toks


def _partial_epilogue(nc, P, sb, sbank, dr):
    j = P.join()
    t_ms = P.do("vector", lambda e: e.memset(sb["osb0"][:, :], 0.0),
                waits=j, inc="dve")
    t_st = P.dma("sync", lambda e: e.dma_start(out=dr["out"][0:128, 0:512],
                                               in_=sb["osb0"][:, :]), [t_ms], "st")
    P.do("sync", lambda e: e.nop(), waits=[t_st])


def build_nc(phases=5):
    nc = bass.Bass("TRN2", target_bir_lowering=False)

    dr = dict(
        xkv=nc.dram_tensor("xkv", [S // 2, D], F32, kind="ExternalInput"),
        xq=nc.dram_tensor("xq", [QR, D], F32, kind="ExternalInput"),
        wq=nc.dram_tensor("wq_t", [NH, 128, NKT, 128], F32, kind="ExternalInput"),
        wk=nc.dram_tensor("wk_t", [NH, 128, NKT, 128], F32, kind="ExternalInput"),
        wv=nc.dram_tensor("wv_t", [NH, 128, NKT, 128], F32, kind="ExternalInput"),
        wo=nc.dram_tensor("wo_t", [NH, 128, D], F32, kind="ExternalInput"),
        wg=nc.dram_tensor("wg_t", [64, 128, NKT, 128], F32, kind="ExternalInput"),
        wu=nc.dram_tensor("wu_t", [64, 128, NKT, 128], F32, kind="ExternalInput"),
        wd=nc.dram_tensor("wd_t", [4, 32, 128, 2, 512], F32, kind="ExternalInput"),
        msk=nc.dram_tensor("msk", [128, NQ, MSK_W], BF16, kind="ExternalInput"),
        eye=nc.dram_tensor("eye", [128, 128], BF16, kind="ExternalInput"),
        out=nc.dram_tensor("out", [QR, D], F32, kind="ExternalOutput"),
        x2s=nc.dram_tensor("x2s", [QR, D], F32),
        kvo=[nc.dram_tensor("kvo%d" % q, [4, 128, 2048], BF16) for q in range(4)],
        kvi=[nc.dram_tensor("kvi%d" % q, [2, 4, 128, 2048], BF16) for q in range(4)],
    )

    P = Plan()
    sb = {}
    es = contextlib.ExitStack()
    with es:
        for name, (off, shape, dt) in SB.items():
            sb[name] = nc.alloc_sbuf_tensor_at(name, shape, dt, offset=off)
        psA = es.enter_context(nc.psum_tensor("psA", [128, 1024], F32))
        psB = es.enter_context(nc.psum_tensor("psB", [128, 1024], F32))
        psC = es.enter_context(nc.psum_tensor("psC", [128, 512], F32))
        psD = es.enter_context(nc.psum_tensor("psD", [128, 512], F32))
        psE = es.enter_context(nc.psum_tensor("psE", [128, 512], F32))
        psF = es.enter_context(nc.psum_tensor("psF", [128, 512], F32))
        sbank = {"A0": (psA, 0), "A1": (psA, 512), "B0": (psB, 0),
                 "B1": (psB, 512), "C": (psC, 0), "D": (psD, 0),
                 "E": (psE, 0), "F": (psF, 0)}

        for nm, owner in (("pe", "tensor"), ("dve", "vector"), ("act", "scalar"),
                          ("cc", "gpsimd")):
            P.add_sem(nm, es.enter_context(nc.semaphore(nm)), owner)
        for g, n in DMA_GROUPS.items():
            owner = "gpsimd" if g == "wld" else "sync"
            for i in range(n):
                nm = "%s%d" % (g, i)
                P.add_sem(nm, es.enter_context(nc.semaphore(nm)), owner)

        emit_program(nc, P, sb, sbank, dr, phases)

        with nc.Block() as block:
            for eng in ENGINES:
                fns = P.ops[eng]
                getattr(block, eng)(lambda e, fns=fns: [f(e) for f in fns])
    return nc


def emit_program(nc, P, sb, sbank, dr, phases=5):
    stats = sb["stats"]
    eye = sb["eye"]

    def bk(name, lo=0, hi=512):
        t, base = sbank[name]
        return t[:, base + lo: base + hi]

    t_eye = P.dma("sync", lambda e: e.dma_start(out=sb["eye"][:, :], in_=dr["eye"][:, :]),
                  [], "ld")
    t_msk = P.dma("sync", lambda e: e.dma_start(out=sb["masks"][:, :, :], in_=dr["msk"][:, :, :]),
                  [], "ld")
    t_eps = P.do("vector", lambda e: e.memset(stats[:, C_EPS:C_EPS + 1], 1e-5), inc="dve")
    t_b44 = P.do("vector", lambda e: e.memset(stats[:, C_EPS + 1:C_EPS + 2], -44.0), inc="dve")
    c44 = stats[:, C_EPS + 1:C_EPS + 2]

    def emit_ln(xt_ap, dst_ap, par, pre_waits, trash_ap):
        neg = stats[:, C_NEG + par: C_NEG + par + 1]
        aux = stats[:, C_AUX + par: C_AUX + par + 1]
        aux2 = stats[:, C_AUX2 + par: C_AUX2 + par + 1]
        inv = stats[:, C_INV + par: C_INV + par + 1]
        t_red = P.do("vector", lambda e: e.tensor_reduce(aux, xt_ap, AXX, AOP.add),
                     waits=pre_waits, inc="dve")
        t_nm = P.do("vector", lambda e: e.tensor_scalar(neg, aux, -1.0 / D, None, AOP.mult),
                    waits=[t_red], inc="dve")
        t_sq = P.do("scalar", lambda e: e.activation(trash_ap, xt_ap, AF.Square,
                                                     bias=neg, scale=1.0, accum_out=aux),
                    waits=[t_nm, t_eps], inc="act")
        t_ln = P.do("scalar", lambda e: e.activation(aux2, aux, AF.Ln,
                                                     bias=stats[:, C_EPS:C_EPS + 1], scale=1.0 / D),
                    waits=[t_sq], inc="act")
        t_inv = P.do("scalar", lambda e: e.activation(inv, aux2, AF.Exp, scale=-0.5),
                     waits=[t_ln], inc="act")
        toks = []
        for q in range(4):
            toks.append(P.do("vector", lambda e, q=q: e.tensor_scalar(
                dst_ap[:, q * 512:(q + 1) * 512], xt_ap[:, q * 512:(q + 1) * 512],
                neg, inv, AOP.add, AOP.mult),
                waits=[t_inv, t_nm], inc="dve"))
        return toks

    cd_free = {"C": None, "D": None}
    ef_free = {"E": None, "F": None}

    def emit_transposes(src, n, dst4_fn, wait_fn, eng="scalar"):
        last = None
        for c in range(n):
            sl = c % 4
            bank = "E" if (c // 4) % 2 == 0 else "F"
            w = wait_fn(c) + [t_eye]
            if sl == 0:
                w.append(ef_free[bank])
            t_mm = P.do("tensor", lambda e, c=c, sl=sl, bank=bank: e.matmul(
                bk(bank, sl * 128, (sl + 1) * 128), src[:, c * 128:(c + 1) * 128],
                eye[:, :], start=True, stop=True),
                waits=w, inc="pe" if sl == 3 else None)
            if sl == 3:
                if eng == "scalar":
                    t_ev = P.do("scalar", lambda e, c=c, bank=bank: e.copy(
                        dst4_fn(c - 3), bk(bank, 0, 512)), waits=[t_mm], inc="act")
                else:
                    t_ev = P.do("vector", lambda e, c=c, bank=bank: e.tensor_copy(
                        dst4_fn(c - 3), bk(bank, 0, 512)), waits=[t_mm], inc="dve")
                ef_free[bank] = t_ev
                last = t_mm
        return last

    xt = [sb["xt0"], sb["xt1"], sb["xt2"], sb["xt3"]]
    htm = [sb["htm0"], sb["htm1"], sb["htm2"], sb["htm3"]]
    xt_free = [None] * 4
    htm_free = [None] * 4
    for qt in range(NQ):
        par = qt % 4
        t_ld = P.dma("sync", lambda e, qt=qt, par=par: e.dma_start(
            out=xt[par][:, :], in_=dr["xq"][qt * 128:(qt + 1) * 128, :]),
            [xt_free[par]], "ld")
        t_norm = emit_ln(xt[par][:, :], htm[par][:, :], par,
                         [t_ld, htm_free[par]], sb["trash"][:, :])
        xt_free[par] = t_norm
        htm_free[par] = emit_transposes(
            htm[par], NKT,
            lambda c0, qt=qt: sb["h1q"][:, c0:c0 + 4, qt * 128:(qt + 1) * 128],
            lambda c: [t_norm[c // 4]])

    t_h1q_done = P.tok("act")

    ab_free = {"A0": None, "A1": None, "B0": None, "B1": None}
    qbuf_free = None

    def emit_q_head(h):
        nonlocal qbuf_free
        t_w = P.dma("gpsimd", lambda e, h=h: e.dma_start(
            out=sb["wqbuf"][:, :, :], in_=dr["wq"][h, :, :, :]),
            [qbuf_free], "wld")
        banks = ("A0", "A1") if (h % 2 == 0) else ("B0", "B1")
        for sp in range(2):
            bank = banks[sp]
            t_mm = None
            for c in range(NKT):
                w = [t_w, t_h1q_done, ab_free[bank]] if c == 0 else []
                t_mm = P.do("tensor", lambda e, h=h, sp=sp, c=c, bank=bank: e.matmul(
                    bk(bank), sb["wqbuf"][:, c, :],
                    sb["h1q"][:, c, sp * 512:(sp + 1) * 512],
                    start=(c == 0), stop=(c == NKT - 1)), waits=w,
                    inc="pe" if c == NKT - 1 else None)
            t_ev = P.do("scalar", lambda e, h=h, sp=sp, bank=bank: e.mul(
                sb["q_all"][:, h, sp * 512:(sp + 1) * 512], bk(bank), ISQ),
                waits=[t_mm], inc="act")
            ab_free[bank] = t_ev
        qbuf_free = P.tok("pe")

    for i in range(8):
        par = i % 4
        t_ld = P.dma("sync", lambda e, i=i, par=par: e.dma_start(
            out=xt[par][:, :], in_=dr["xkv"][i * 128:(i + 1) * 128, :]),
            [xt_free[par]], "ld")
        t_norm = emit_ln(xt[par][:, :], htm[par][:, :], par,
                         [t_ld, htm_free[par]], sb["trash"][:, :])
        xt_free[par] = t_norm
        htm_free[par] = emit_transposes(
            htm[par], NKT,
            lambda c0, i=i: sb["h1_fm"][:, c0:c0 + 4, i * 128:(i + 1) * 128],
            lambda c: [t_norm[c // 4]])
        emit_q_head(2 * i)
        emit_q_head(2 * i + 1)

    t_h1_done = P.tok("act")
    if phases <= 1:
        return _partial_epilogue(nc, P, sb, sbank, dr)

    kfm = [sb["kfm0"], sb["kfm1"]]
    vtm = [sb["vtm0"], sb["vtm1"]]
    pt = [sb["pt0"], sb["pt1"], sb["pt2"], sb["pt3"]]
    wk_dma_tok = [None] * NH
    wv_dma_tok = [None] * NH
    kv_ev = [None] * 4
    k_in_tok = [[None] * 2, [None] * 2]
    v_in_tok = [[None] * 2, [None] * 2]
    kfm_free = [None, None]
    stg_free = None
    kbuf_free = None
    vbuf_free = None
    vfm_free = None
    pt_free = [None, None, None, None]
    grp_free = [None, None]
    grp_exp = [[], []]
    p_region_free = [None] * NQ
    diag_free = [None] * NQ
    exp_tok = [None] * NQ
    diag_tok = [None] * NQ

    def emit_wk_dma(h):
        wk_dma_tok[h] = P.dma("gpsimd", lambda e, h=h: e.dma_start(
            out=sb["wkbuf"][:, :, :], in_=dr["wk"][h, :, :, :]),
            [kbuf_free], "wld")

    def emit_wv_dma(h):
        wv_dma_tok[h] = P.dma("gpsimd", lambda e, h=h: e.dma_start(
            out=sb["wvbuf"][:, :, :], in_=dr["wv"][h, :, :, :]),
            [vbuf_free], "wld")

    def emit_k(h):
        nonlocal kbuf_free
        for sp in range(2):
            bank = SBANKS[sp]
            t_mm = None
            for c in range(NKT):
                w = [wk_dma_tok[h], t_h1_done, ab_free[bank]] if c == 0 else []
                t_mm = P.do("tensor", lambda e, h=h, sp=sp, c=c, bank=bank: e.matmul(
                    bk(bank), sb["wkbuf"][:, c, :],
                    sb["h1_fm"][:, c, sp * 512:(sp + 1) * 512],
                    start=(c == 0), stop=(c == NKT - 1)), waits=w,
                    inc="pe" if c == NKT - 1 else None)
            t_ev = P.do("vector", lambda e, sp=sp, bank=bank: e.tensor_copy(
                sb["kvstg"][:, sp * 512:(sp + 1) * 512], bk(bank)),
                waits=[t_mm, stg_free], inc="dve")
            kv_ev[sp] = t_ev
            ab_free[bank] = t_ev
        kbuf_free = P.tok("pe")

    def emit_v(h):
        nonlocal vbuf_free, stg_free
        for sp in range(2):
            bank = SBANKS[2 + sp]
            t_mm = None
            for c in range(NKT):
                w = [wv_dma_tok[h], ab_free[bank]] if c == 0 else []
                t_mm = P.do("tensor", lambda e, h=h, sp=sp, c=c, bank=bank: e.matmul(
                    bk(bank), sb["wvbuf"][:, c, :],
                    sb["h1_fm"][:, c, sp * 512:(sp + 1) * 512],
                    start=(c == 0), stop=(c == NKT - 1)), waits=w,
                    inc="pe" if c == NKT - 1 else None)
            t_ev = P.do("vector", lambda e, sp=sp, bank=bank: e.tensor_copy(
                sb["kvstg"][:, 1024 + sp * 512:1024 + (sp + 1) * 512], bk(bank)),
                waits=[t_mm, stg_free], inc="dve")
            kv_ev[2 + sp] = t_ev
            ab_free[bank] = t_ev
        vbuf_free = P.tok("pe")
        stg_free = P.dma("sync", lambda e, h=h: e.dma_start(
            out=dr["kvo"][h // 4][h % 4, :, :], in_=sb["kvstg"][:, :]),
            list(kv_ev), "st")

    def emit_kv_in(h):
        par = h % 2
        for r in range(2):
            k_in_tok[par][r] = P.dma("sync", lambda e, r=r, par=par, h=h: e.dma_start(
                out=kfm[par][:, r * 1024:(r + 1) * 1024],
                in_=dr["kvi"][h // 4][r, h % 4, :, 0:1024]),
                [t_cc[h // 4], kfm_free[par]], "ld")
            v_in_tok[par][r] = P.dma("sync", lambda e, r=r, par=par, h=h: e.dma_start(
                out=sb["vfm"][:, r * 1024:(r + 1) * 1024],
                in_=dr["kvi"][h // 4][r, h % 4, :, 1024:2048]),
                [t_cc[h // 4], vfm_free], "ld")

    def emit_vt(h):
        nonlocal vfm_free
        vfm_free = emit_transposes(
            sb["vfm"], NKT,
            lambda j0, h=h: vtm[h % 2][:, j0:j0 + 4, :],
            lambda j, h=h: [v_in_tok[h % 2][j // 8]], eng="vector")

    def emit_scores_softmax(h, s):
        K = KLENS[s]
        sfx = K - MSK_W
        g = s // 4
        nch = 4 if g == 0 else 2
        sstat = sb["sstat"]
        mm_toks = []
        for c in range(nch):
            lo = c * 512
            hi = min(K, lo + 512)
            bank = SBANKS[c]
            w = [ab_free[bank], t_msk, t_eye, k_in_tok[h % 2][c // 2]]
            masked = hi > sfx
            t_mm = P.do("tensor", lambda e, h=h, s=s, lo=lo, hi=hi, bank=bank, masked=masked: e.matmul(
                bk(bank, 0, hi - lo), sb["q_all"][:, h, s * 128:(s + 1) * 128],
                kfm[h % 2][:, lo:hi], start=True, stop=not masked),
                waits=w, inc=None if masked else "pe")
            if masked:
                mlo = max(lo, sfx)
                t_mm = P.do("tensor", lambda e, s=s, mlo=mlo, hi=hi, lo=lo, sfx=sfx, bank=bank: e.matmul(
                    bk(bank, mlo - lo, hi - lo), eye[:, :],
                    sb["masks"][:, s, mlo - sfx:hi - sfx],
                    start=False, stop=True), inc="pe")
            mm_toks.append(t_mm)
        nsec = (nch + 1) // 2
        for sec in range(nsec):
            slo = sec * 1024
            shi = min(K, slo + 1024)
            sc_t = sbank[SBANKS[2 * sec]][0]
            t_exp = P.do("scalar", lambda e, s=s, slo=slo, shi=shi, sec=sec, sc_t=sc_t, sstat=sstat: e.activation(
                sb["p_sb"][:, P_OFF[s] + slo:P_OFF[s] + shi], sc_t[:, 0:shi - slo],
                AF.Exp, bias=c44, scale=1.0,
                accum_out=sstat[:, 48 + s * 2 + sec:49 + s * 2 + sec]),
                waits=[mm_toks[2 * sec], mm_toks[2 * sec + 1], t_b44,
                       p_region_free[s], grp_free[g]], inc="act")
            grp_exp[g].append(t_exp)
            ab_free[SBANKS[2 * sec]] = t_exp
            ab_free[SBANKS[2 * sec + 1]] = t_exp
        exp_tok[s] = grp_exp[g][-1]

    def emit_combine(h, g):
        sstat = sb["sstat"]
        st_t = sb["stats"]
        if g == 0:
            sums3 = bass.AP(sstat, 48, [[96, 128], [2, 4], [1, 2]])
            sgc = st_t[:, C_SG2:C_SG2 + 4]
            rgc = st_t[:, C_RG:C_RG + 4]
            t_sg = P.do("vector", lambda e, sums3=sums3, sgc=sgc: e.tensor_reduce(
                sgc, sums3, AXX, AOP.add), waits=grp_exp[g] + [grp_free[g]], inc="dve")
            t_rc = P.do("vector", lambda e, sgc=sgc, rgc=rgc: e.reciprocal(rgc, sgc),
                        waits=[t_sg], inc="dve")
        else:
            ssums = bass.AP(sstat, 56, [[96, 128], [2, 4]])
            rgc = st_t[:, C_RG + 4:C_RG + 8]
            t_rc = P.do("vector", lambda e, ssums=ssums, rgc=rgc: e.reciprocal(rgc, ssums),
                        waits=grp_exp[g] + [grp_free[g]], inc="dve")
        dtok = None
        for si in range(4):
            s = 4 * g + si
            dtok = P.do("vector", lambda e, s=s, si=si, rgc=rgc: e.tensor_scalar(
                sb["diag"][:, s, 0, :], eye[:, :], rgc[:, si:si + 1], None, AOP.mult),
                waits=[t_rc, t_eye, diag_free[s]], inc="dve")
            diag_tok[s] = dtok
        grp_free[g] = dtok
        grp_exp[g] = []

    def tpv_steps(h, g, slots):
        obank = "E" if g == 0 else "F"
        maxJ = JS[slots[0]]
        pend_pv = []
        steps = []

        def step_j(j):
            valid = [s for s in slots if JS[s] > j]
            nv = len(valid)
            jpar = j % 4
            bank = "C" if j % 2 == 0 else "D"

            t_pt = None
            for sl, s in enumerate(valid):
                w = [diag_tok[s], exp_tok[s]]
                if sl == 0:
                    w.append(cd_free[bank])
                t_pt = P.do("tensor", lambda e, s=s, j=j, sl=sl, bank=bank: e.matmul(
                    bk(bank, sl * 128, (sl + 1) * 128),
                    sb["p_sb"][:, P_OFF[s] + j * 128:P_OFF[s] + (j + 1) * 128],
                    sb["diag"][:, s, 0, :], start=True, stop=True), waits=w,
                    inc="pe" if sl == nv - 1 else None)
            for s in valid:
                p_region_free[s] = t_pt
                diag_free[s] = t_pt
            if j % 2 == 0:
                t_ev = P.do("scalar", lambda e, jpar=jpar, nv=nv, bank=bank: e.copy(
                    pt[jpar][:, 0:nv * 128], bk(bank, 0, nv * 128)),
                    waits=[t_pt, pt_free[jpar]], inc="act")
            else:
                t_ev = P.do("vector", lambda e, jpar=jpar, nv=nv, bank=bank: e.tensor_copy(
                    pt[jpar][:, 0:nv * 128], bk(bank, 0, nv * 128)),
                    waits=[t_pt, pt_free[jpar]], inc="dve")
            cd_free[bank] = t_ev
            w = [t_ev]
            if j == 0:
                w.append(ef_free[obank])

            def _pv(j=j, nv=nv, jpar=jpar, w=tuple(w)):
                t_pv = P.do("tensor", lambda e, h=h, j=j, nv=nv, jpar=jpar, obank=obank, maxJ=maxJ: e.matmul(
                    bk(obank, 0, nv * 128), vtm[h % 2][:, j, :],
                    pt[jpar][:, 0:nv * 128], start=(j == 0), stop=(j == maxJ - 1)),
                    waits=list(w), inc="pe")
                pt_free[jpar] = t_pv
            if len(pend_pv) >= 2:
                pend_pv.pop(0)()
            pend_pv.append(_pv)

        def tail():
            while pend_pv:
                pend_pv.pop(0)()
            t_oev = P.do("vector", lambda e, h=h, g=g, obank=obank: e.tensor_copy(
                sb["o_fm"][:, h, g * 512:(g + 1) * 512], bk(obank)),
                waits=[P.tok("pe")], inc="dve")
            ef_free[obank] = t_oev

        for j in range(maxJ):
            steps.append(lambda j=j: step_j(j))
        steps.append(tail)
        return steps

    t_cc = [None] * 4
    for h in range(NH):
        emit_wk_dma(h)
        emit_k(h)
        emit_wv_dma(h)
        emit_v(h)
        if h % 4 == 3:
            q4 = h // 4
            t_cc[q4] = P.do("gpsimd", lambda e, q4=q4: e.collective_compute(
                "AllGather", AOP.bypass,
                replica_groups=[[0, 1], [2, 3], [4, 5], [6, 7]],
                ins=[dr["kvo"][q4][:, :, :]],
                outs=[dr["kvi"][q4][:, :, :, :]]),
                waits=[stg_free, P.tok("st0"), P.tok("st1")], inc="cc")
    emit_kv_in(0)
    pend = []

    def drain(k):
        for _ in range(min(k, len(pend))):
            pend.pop(0)()

    for h in range(NH):
        for i, s in enumerate((0, 1, 2, 3)):
            emit_scores_softmax(h, s)
            drain(3)
        emit_combine(h, 0)
        drain(99)
        emit_vt(h)
        pend = tpv_steps(h, 0, [0, 1, 2, 3])
        for i, s in enumerate((4, 5, 6, 7)):
            emit_scores_softmax(h, s)
            drain(5)
        emit_combine(h, 1)
        kfm_free[h % 2] = P.tok("pe")
        if h + 1 < NH:
            emit_kv_in(h + 1)
        drain(99)
        pend = tpv_steps(h, 1, [4, 5, 6, 7])
    drain(99)

    j_p2 = P.join()
    if phases <= 2:
        return _partial_epilogue(nc, P, sb, sbank, dr)

    wo_dma_tok = [None] * NH
    for of in range(NH):
        wo_dma_tok[of] = P.dma("gpsimd", lambda e, of=of: e.dma_start(
            out=sb["woT"][:, of, :], in_=dr["wo"][of, :, :]),
            j_p2 if of == 0 else [], "wld")

    xot = [sb["xot0"], sb["xot1"]]
    x2t = [sb["x2t0"], sb["x2t1"]]
    h2tm = [sb["h2tm0"], sb["h2tm1"]]
    xot_free = [None, None]
    x2t_free = [[], []]
    h2tm_free = [None, None]
    h2T_done = [None, None]
    wo_banks = ["A0", "A1", "B0", "B1", "C", "D", "E", "F"]
    wo_bank_free = {b: None for b in wo_banks}
    x2st_tok = [None] * NQ
    pend_T = []

    def emit_h2T(slot, par):
        for c in range(NKT):
            bank = "C" if (c % 2 == 0) else "D"
            t_mm = P.do("tensor", lambda e, c=c, par=par, bank=bank: e.matmul(
                bk(bank, 0, 128), h2tm[par][:, c * 128:(c + 1) * 128],
                eye[:, :], start=True, stop=True),
                waits=[h2tm_free[par], wo_bank_free[bank]], inc="pe")
            t_ev = P.do("scalar", lambda e, c=c, slot=slot, bank=bank: e.copy(
                sb["h2_fm"][:, c, slot * 128:(slot + 1) * 128],
                bk(bank, 0, 128)), waits=[t_mm], inc="act")
            wo_bank_free[bank] = t_ev
        h2T_done[par] = P.tok("pe")

    def emit_wo_slot(slot, idx, first):
        banks = wo_banks[idx * 4:(idx + 1) * 4]
        t_mm = None
        for of in range(NH):
            for q in range(4):
                bank = banks[q]
                w = [wo_dma_tok[of]] if q == 0 else []
                if of == 0:
                    w = w + [wo_bank_free[bank]] + (j_p2 if first else [])
                t_mm = P.do("tensor", lambda e, of=of, q=q, slot=slot, bank=bank: e.matmul(
                    bk(bank), sb["o_fm"][:, of, slot * 128:(slot + 1) * 128],
                    sb["woT"][:, of, q * 512:(q + 1) * 512],
                    start=(of == 0), stop=(of == NH - 1)), waits=w,
                    inc="pe" if (of == NH - 1 and q == 3) else None)
        return t_mm, banks

    def emit_wo_epilogue(slot, banks):
        par = slot % 2
        ev_toks = []
        for q in range(4):
            bank = banks[q]
            t_ev = P.do("vector", lambda e, q=q, par=par, bank=bank: e.tensor_add(
                x2t[par][:, q * 512:(q + 1) * 512], bk(bank),
                xot[par][:, q * 512:(q + 1) * 512]),
                waits=[t_wo_mm[slot], xot_free[par]] + x2t_free[par], inc="dve")
            wo_bank_free[bank] = t_ev
            ev_toks.append(t_ev)
        xot_free[par] = ev_toks[-1]
        x2st_tok[slot] = P.dma("sync", lambda e, slot=slot, par=par: e.dma_start(
            out=dr["x2s"][slot * 128:(slot + 1) * 128, :], in_=x2t[par][:, :]),
            ev_toks, "st")
        t_norm = emit_ln(x2t[par][:, :], h2tm[par][:, :], par,
                         ev_toks + [h2T_done[par]], sb["trash3"][:, :])
        x2t_free[par] = t_norm + [x2st_tok[slot]]
        h2tm_free[par] = t_norm
        pend_T.append((slot, par))

    t_wo_mm = {}
    for r in range(4):
        slots = (2 * r, 2 * r + 1)
        for slot in slots:
            par = slot % 2
            xot_free[par] = P.dma("sync", lambda e, slot=slot, par=par: e.dma_start(
                out=xot[par][:, :], in_=dr["xq"][slot * 128:(slot + 1) * 128, :]),
                [xot_free[par]] + (j_p2 if r == 0 else []), "ld")
        t_wo_mm[slots[0]], banks0 = emit_wo_slot(slots[0], 0, r == 0)
        if r > 0:
            for slot, par in pend_T[:2]:
                emit_h2T(slot, par)
            del pend_T[:2]
        emit_wo_epilogue(slots[0], banks0)
        t_wo_mm[slots[1]], banks1 = emit_wo_slot(slots[1], 1, False)
        emit_wo_epilogue(slots[1], banks1)
    for slot, par in pend_T:
        emit_h2T(slot, par)
    pend_T.clear()

    j_p3 = P.join()
    if phases <= 3:
        return _partial_epilogue(nc, P, sb, sbank, dr)

    wg = [sb["wg0"], sb["wg1"]]
    wu = [sb["wu0"], sb["wu1"]]
    wg_dma = [None] * 64
    wu_dma = [None] * 64
    wg_free = [None, None]
    wu_free = [None, None]
    gate_banks = [("A0", "A1"), ("C", "D")]
    up_banks = [("B0", "B1"), ("E", "F")]
    gbank_free = {b: None for b in wo_banks}
    hid_tok = [None] * 64
    silu_free = [None, None]

    def emit_wgu_dma(hc):
        par = hc % 2
        wg_dma[hc] = P.dma("gpsimd", lambda e, hc=hc, par=par: e.dma_start(
            out=wg[par][:, :, :], in_=dr["wg"][hc, :, :, :]),
            [wg_free[par]] + (j_p3 if hc < 2 else []), "wld")
        wu_dma[hc] = P.dma("gpsimd", lambda e, hc=hc, par=par: e.dma_start(
            out=wu[par][:, :, :], in_=dr["wu"][hc, :, :, :]),
            [wu_free[par]], "wld")

    emit_wgu_dma(0)
    emit_wgu_dma(1)
    for hc in range(64):
        par = hc % 2
        gb = gate_banks[par]
        ub = up_banks[par]
        t_gmm = None
        for c in range(NKT):
            for half in range(2):
                w = []
                if c == 0 and half == 0:
                    w = [wg_dma[hc], j_p3, gbank_free[gb[0]], gbank_free[gb[1]]]
                t_gmm = P.do("tensor", lambda e, hc=hc, c=c, half=half, par=par, gb=gb: e.matmul(
                    bk(gb[half]), wg[par][:, c, :],
                    sb["h2_fm"][:, c, half * 512:(half + 1) * 512],
                    start=(c == 0), stop=(c == NKT - 1)), waits=w,
                    inc="pe" if (c == NKT - 1 and half == 1) else None)
        t_umm = None
        for c in range(NKT):
            for half in range(2):
                w = []
                if c == 0 and half == 0:
                    w = [wu_dma[hc], gbank_free[ub[0]], gbank_free[ub[1]]]
                t_umm = P.do("tensor", lambda e, hc=hc, c=c, half=half, par=par, ub=ub: e.matmul(
                    bk(ub[half]), wu[par][:, c, :],
                    sb["h2_fm"][:, c, half * 512:(half + 1) * 512],
                    start=(c == 0), stop=(c == NKT - 1)), waits=w,
                    inc="pe" if (c == NKT - 1 and half == 1) else None)
        wg_free[par] = t_gmm
        wu_free[par] = t_umm
        if hc + 2 < 64:
            emit_wgu_dma(hc + 2)
        t_silu = [None, None]
        for half in range(2):
            t_silu[half] = P.do("scalar", lambda e, half=half, gb=gb: e.activation(
                sb["silu"][:, half * 512:(half + 1) * 512],
                bk(gb[half]), AF.Silu),
                waits=[t_gmm, silu_free[half]], inc="act")
            gbank_free[gb[half]] = t_silu[half]
        muls = []
        for half in range(2):
            t_mul = P.do("vector", lambda e, hc=hc, half=half, ub=ub: e.tensor_tensor(
                sb["hidden"][:, hc, half * 512:(half + 1) * 512],
                sb["silu"][:, half * 512:(half + 1) * 512],
                bk(ub[half]), AOP.mult),
                waits=[t_umm, t_silu[half], j_p3], inc="dve")
            gbank_free[ub[half]] = t_mul
            silu_free[half] = t_mul
            muls.append(t_mul)
        hid_tok[hc] = muls[-1]

    j_p4 = P.join()
    if phases <= 4:
        return _partial_epilogue(nc, P, sb, sbank, dr)

    wd = [sb["wd0"], sb["wd1"], sb["wd2"], sb["wd3"]]
    x2r = [sb["x2r0"], sb["x2r1"]]
    osb = [sb["osb0"], sb["osb1"]]
    wd_free = [None, None, None, None]
    x2r_free = [None, None]
    osb_free = [None, None]
    dbank_free = {b: None for b in wo_banks}
    wd_dma = {}
    x2r_dma = {}

    def emit_wd_dma(qtr, pair):
        par = pair % 4
        wd_dma[(qtr, pair)] = P.dma("gpsimd", lambda e, qtr=qtr, pair=pair, par=par: e.dma_start(
            out=wd[par][:, :, :], in_=dr["wd"][qtr, pair, :, :, :]),
            [wd_free[par]], "wld")

    def emit_x2r_dma(qtr, tt):
        par = tt % 2
        x2r_dma[(qtr, tt)] = P.dma("sync", lambda e, qtr=qtr, tt=tt, par=par: e.dma_start(
            out=x2r[par][:, :], in_=dr["x2s"][tt * 128:(tt + 1) * 128,
                                              qtr * 512:(qtr + 1) * 512]),
            [x2r_free[par], x2st_tok[tt]], "ld")

    for k in range(4):
        emit_wd_dma(0, k)
    emit_x2r_dma(0, 0)
    emit_x2r_dma(0, 1)
    for qtr in range(4):
        for hc in range(64):
            par = (hc // 2) % 4
            t_mm = None
            for tt in range(8):
                bank = wo_banks[tt]
                w = []
                if tt == 0:
                    w.append(wd_dma[(qtr, hc // 2)])
                    if qtr == 0:
                        w.append(hid_tok[hc])
                        if hc == 0:
                            w.extend(j_p4)
                if hc == 0:
                    w.append(dbank_free[bank])
                t_mm = P.do("tensor", lambda e, qtr=qtr, hc=hc, tt=tt, par=par, bank=bank: e.matmul(
                    bk(bank), sb["hidden"][:, hc, tt * 128:(tt + 1) * 128],
                    wd[par][:, hc % 2, :], start=(hc == 0), stop=(hc == 63)), waits=w,
                    inc="pe" if tt == 7 else None)
            if hc % 2 == 1:
                wd_free[par] = t_mm
                pair = hc // 2
                nxt = (qtr, pair + 4) if pair + 4 < 32 else (qtr + 1, pair + 4 - 32)
                if nxt[0] < 4:
                    emit_wd_dma(*nxt)
        t_mm_all = P.tok("pe")
        for tt in range(8):
            bank = wo_banks[tt]
            par = tt % 2
            t_ev = P.do("vector", lambda e, tt=tt, par=par, bank=bank: e.tensor_add(
                osb[par][:, :], bk(bank), x2r[par][:, :]),
                waits=[t_mm_all, x2r_dma[(qtr, tt)], osb_free[par]], inc="dve")
            dbank_free[bank] = t_ev
            x2r_free[par] = t_ev
            t_st = P.dma("sync", lambda e, qtr=qtr, tt=tt, par=par: e.dma_start(
                out=dr["out"][tt * 128:(tt + 1) * 128, qtr * 512:(qtr + 1) * 512],
                in_=osb[par][:, :]), [t_ev], "st")
            osb_free[par] = t_st
            nq = (qtr, tt + 2) if tt + 2 < 8 else (qtr + 1, tt + 2 - 8)
            if nq[0] < 4:
                emit_x2r_dma(*nq)

    P.do("sync", lambda e: e.nop(),
         waits=[P.tok("st%d" % i) for i in range(DMA_GROUPS["st"])])


_NC_CACHE = {}


def _get_nc():
    if "nc" not in _NC_CACHE:
        _NC_CACHE["nc"] = build_nc()
    return _NC_CACHE["nc"]


def _host_prep(wq, wk, wv, wo, wg, wu, wd):
    def qkv_tile(w):
        return np.ascontiguousarray(
            w.reshape(NH, 128, NKT, 128).transpose(0, 3, 2, 1))

    wq_t = qkv_tile(wq)
    wk_t = qkv_tile(wk)
    wv_t = qkv_tile(wv)
    wo_t = np.ascontiguousarray(wo.reshape(D, NH, 128).transpose(1, 2, 0))
    wg_t = np.ascontiguousarray(wg.reshape(64, 128, NKT, 128).transpose(0, 3, 2, 1))
    wu_t = np.ascontiguousarray(wu.reshape(64, 128, NKT, 128).transpose(0, 3, 2, 1))
    wd_t = np.ascontiguousarray(
        wd.reshape(4, 512, 32, 2, 128).transpose(0, 2, 4, 3, 1))
    return wq_t, wk_t, wv_t, wo_t, wg_t, wu_t, wd_t


def _masks_for_class(cls):
    m = np.zeros((128, NQ, MSK_W), np.float32)
    for s in range(NQ):
        t = SLOT_TILES[cls][s]
        K = KLENS[s]
        sfx = K - MSK_W
        for jj in range(MSK_W):
            col = sfx + jj
            bad = col - t * 128
            if bad > 0:
                m[:min(bad, 128), s, jj] = -1e9
    return m.astype(BF)


def kernel(L0_attn_x, L0_attn_wq, L0_attn_wk, L0_attn_wv, L0_attn_wo,
           L0_ffn_w_gate, L0_ffn_w_up, L0_ffn_w_down):
    x = np.ascontiguousarray(np.asarray(L0_attn_x, dtype=np.float32))
    wq = np.asarray(L0_attn_wq, dtype=np.float32)
    wk = np.asarray(L0_attn_wk, dtype=np.float32)
    wv = np.asarray(L0_attn_wv, dtype=np.float32)
    wo = np.asarray(L0_attn_wo, dtype=np.float32)
    wg = np.asarray(L0_ffn_w_gate, dtype=np.float32)
    wu = np.asarray(L0_ffn_w_up, dtype=np.float32)
    wd = np.asarray(L0_ffn_w_down, dtype=np.float32)

    wq_t, wk_t, wv_t, wo_t, wg_t, wu_t, wd_t = _host_prep(wq, wk, wv, wo, wg, wu, wd)
    eye = np.eye(128, dtype=np.float32).astype(BF)
    masks = [_masks_for_class(0), _masks_for_class(1)]
    perms = []
    for cls in range(2):
        perms.append(np.concatenate(
            [np.arange(t * 128, (t + 1) * 128) for t in SLOT_TILES[cls]]))

    nc = _get_nc()
    in_maps = []
    for c in range(8):
        b, cls = c // 2, c % 2
        in_maps.append({
            "xkv": np.ascontiguousarray(x[b][(c % 2) * 1024:(c % 2) * 1024 + 1024]),
            "xq": np.ascontiguousarray(x[b][perms[cls]]),
            "wq_t": wq_t, "wk_t": wk_t, "wv_t": wv_t, "wo_t": wo_t,
            "wg_t": wg_t, "wu_t": wu_t, "wd_t": wd_t,
            "msk": masks[cls], "eye": eye,
        })
    res = run_bass_kernel_spmd(nc, in_maps, core_ids=list(range(8)))
    out = np.empty((B, S, D), np.float32)
    for c in range(8):
        b, cls = c // 2, c % 2
        out[b][perms[cls]] = res.results[c]["out"]
    return out
